# version 13 (final)
# baseline (speedup 1.0000x reference)
import numpy as np
import ml_dtypes
import concourse.bass as bass
import concourse.mybir as mybir
from concourse.bass_utils import run_bass_kernel_spmd

F32 = mybir.dt.float32
BF16 = mybir.dt.bfloat16
AF = mybir.ActivationFunctionType
ALU = mybir.AluOpType
AX = mybir.AxisListType

T, D, DFF = 8192, 1024, 4096
NSPAN, SPAN = 16, 512
NTILE = T // 128
NBLK = 16
G = 4
NG = NTILE // G
FSPAN, NFSPAN = 256, 32
INV32 = 1.0 / 32.0
EPS = 1e-5
NWELEM = float(DFF * D)

ENGINES = ("pe", "act", "dve", "pool", "sp")
QUEUES = ("qsp", "qpool", "qact")
QENG = {"qsp": "sp", "qpool": "pool", "qact": "act"}


class _Op:
    __slots__ = ("engine", "fn", "hazard", "dq", "deps", "marked", "inc_idx",
                 "dma_idx", "waits", "seq")

    def __init__(self, engine, fn, hazard, dq, seq):
        self.engine = engine
        self.fn = fn
        self.hazard = hazard
        self.dq = dq
        self.deps = set()
        self.marked = False
        self.inc_idx = None
        self.dma_idx = None
        self.waits = []
        self.seq = seq


class Prog:

    def __init__(self):
        self.ops = []
        self.spaces = {}

    def _segs(self, sp):
        return self.spaces.setdefault(sp, [])

    def op(self, engine, fn, reads=(), writes=(), hazard=False, dq=None):
        if dq is None and engine in ("dve", "pool"):
            hazard = True
        o = _Op(engine, fn, hazard, dq, len(self.ops))
        deps = o.deps
        for (sp, lo, hi) in reads:
            segs = self._segs(sp)
            for s in segs:
                if s[0] < hi and lo < s[1]:
                    if s[2] is not None:
                        deps.add(s[2])
                    s[3].append(o)
        for (sp, lo, hi) in writes:
            segs = self._segs(sp)
            out = []
            for s in segs:
                if s[0] < hi and lo < s[1]:
                    if s[2] is not None:
                        deps.add(s[2])
                    deps.update(s[3])
                    if s[0] < lo:
                        out.append([s[0], lo, s[2], list(s[3])])
                    if hi < s[1]:
                        out.append([hi, s[1], s[2], list(s[3])])
                else:
                    out.append(s)
            out.append([lo, hi, o, []])
            self.spaces[sp] = out
        deps.discard(o)
        self.ops.append(o)
        return o

    def finalize(self):
        for o in self.ops:
            for p in o.deps:
                if p.dq is None and (p.engine != o.engine or p.hazard):
                    p.marked = True
        cnt = {e: 0 for e in ENGINES}
        dcnt = {q: 0 for q in QUEUES}
        for o in self.ops:
            if o.dq is not None:
                o.dma_idx = dcnt[o.dq]
                dcnt[o.dq] += 1
            elif o.marked:
                cnt[o.engine] += 1
                o.inc_idx = cnt[o.engine]
        watermark = {e: {} for e in ENGINES}
        for o in self.ops:
            req = {}
            for p in o.deps:
                if p.dq is not None:
                    key, val = p.dq, 16 * (p.dma_idx + 1)
                elif p.engine != o.engine or p.hazard:
                    key, val = p.engine, p.inc_idx
                else:
                    continue
                if req.get(key, 0) < val:
                    req[key] = val
            wm = watermark[o.engine]
            for k in sorted(req):
                v = req[k]
                if wm.get(k, 0) < v:
                    o.waits.append((k, v))
                    wm[k] = v

    def emit(self, nc):
        sems = {}
        for e in ENGINES:
            sems[e] = nc.alloc_semaphore("sem_" + e)
        for q in QUEUES:
            sems[q] = nc.alloc_semaphore("sem_" + q)
        by_engine = {e: [o for o in self.ops if o.engine == e] for e in ENGINES}

        def runner(ename):
            def body(eng):
                for o in by_engine[ename]:
                    for (k, v) in o.waits:
                        eng.wait_ge(sems[k], v)
                    r = o.fn(eng)
                    if isinstance(r, (list, tuple)):
                        r = r[-1]
                    if o.dq is not None:
                        r.then_inc(sems[o.dq], 16)
                    elif o.marked:
                        r.then_inc(sems[o.engine], 1)
            return body

        with nc.Block() as block:
            block.tensor(runner("pe"))
            block.scalar(runner("act"))
            block.vector(runner("dve"))
            block.gpsimd(runner("pool"))
            block.sync(runner("sp"))


class Buf:
    __slots__ = ("th", "space", "off", "size")

    def __init__(self, th, space, off, size):
        self.th = th
        self.space = space
        self.off = off
        self.size = size

    def r(self, lo=0, hi=None):
        if hi is None:
            hi = self.size
        return (self.space, self.off + lo, self.off + hi)


class SbufAlloc:
    def __init__(self, nc, base, top):
        self.nc = nc
        self.base = base
        self.top = top
        self.cur = base

    def alloc(self, name, shape, dtype):
        nbytes = int(np.prod(shape[1:])) * mybir.dt.size(dtype)
        off = (self.cur + 63) & ~63
        assert off + nbytes <= self.top, (
            f"SBUF overflow at {name}: {off}+{nbytes} > {self.top}")
        th = self.nc.alloc_sbuf_tensor_at(name, list(shape), dtype, offset=off)
        self.cur = off + nbytes
        return Buf(th, "sbuf", off, nbytes)


def psum_at(nc, name, bank, shape, dtype):
    nbytes = int(np.prod(shape[1:])) * mybir.dt.size(dtype)
    nc.psum_base = bank
    th = nc.alloc_psum_tensor(name, list(shape), dtype)
    return Buf(th, "psum", bank * 2048, nbytes)


def dr(name, lo, hi):
    return ("dram_" + name, lo, hi)


def build():
    nc = bass.Bass("TRN2", target_bir_lowering=False, debug=False)
    P = Prog()

    x_d = nc.dram_tensor("x", [T, D], F32, kind="ExternalInput")
    wqt_d = nc.dram_tensor("wqt", [D, D], F32, kind="ExternalInput")
    wkt_d = nc.dram_tensor("wkt", [D, D], F32, kind="ExternalInput")
    wvt_d = nc.dram_tensor("wvt", [D, D], F32, kind="ExternalInput")
    w1t_d = nc.dram_tensor("w1t", [D, DFF], F32, kind="ExternalInput")
    w2t_d = nc.dram_tensor("w2t", [DFF, D], F32, kind="ExternalInput")
    b1m_d = nc.dram_tensor("b1m", [128, 32], F32, kind="ExternalInput")
    b2_d = nc.dram_tensor("b2", [1, D], F32, kind="ExternalInput")
    out_d = nc.dram_tensor("out", [T, D], F32, kind="ExternalOutput")

    qt_s = nc.dram_tensor("qt_s", [NSPAN, 128, 8, SPAN], BF16, kind="Internal")
    kt_s = nc.dram_tensor("kt_s", [NSPAN, 128, 8, SPAN], BF16, kind="Internal")
    v_s = nc.dram_tensor("v_s", [128, NTILE, D], BF16, kind="Internal")
    x1_s = nc.dram_tensor("x1_s", [T, D], F32, kind="Internal")
    w1q_s = nc.dram_tensor("w1q_s", [8, 128, DFF], BF16, kind="Internal")
    w2q_s = nc.dram_tensor("w2q_s", [32, 128, D], BF16, kind="Internal")

    ident_np = np.eye(128, dtype=np.float32).astype(ml_dtypes.bfloat16)
    ident_d = nc.inline_tensor(ident_np, name="identb")
    ones_r_d = nc.inline_tensor(np.ones((1, 128), dtype=np.float32), name="ones_r")
    ones_c_d = nc.inline_tensor(np.ones((128, 1), dtype=np.float32), name="ones_c")

    x_r = x_d.ap().rearrange("(n p) d -> p n d", p=128)
    x1_r = x1_s.ap().rearrange("(n p) d -> p n d", p=128)
    out_r = out_d.ap().rearrange("(n p) d -> p n d", p=128)
    wqt_r = wqt_d.ap().rearrange("(c p) o -> p c o", p=128)
    wkt_r = wkt_d.ap().rearrange("(c p) o -> p c o", p=128)
    wvt_r = wvt_d.ap().rearrange("(c p) o -> p c o", p=128)
    w1t_r = w1t_d.ap().rearrange("(c p) f -> p c f", p=128)
    w2t_r = w2t_d.ap().rearrange("(c p) d -> p c d", p=128)
    w2q_r = w2q_s.ap().rearrange("c p d -> p c d")

    TILB = 128 * D * 4
    TILB2 = 128 * D * 2
    QTB = 128 * 1024 * 2
    SPANB = 128 * SPAN * 2
    KSPANB = 8 * SPANB

    sb_base = (nc.sbuf_base + 63) & ~63
    sb_top = nc.sbuf_top
    A = SbufAlloc(nc, sb_base, sb_top)

    identb = A.alloc("identb", [128, 128], BF16)
    ones_r = A.alloc("ones_r", [1, 128], F32)
    ones_c = A.alloc("ones_c", [128, 1], F32)
    consts = A.alloc("consts", [128, 2], F32)
    nc.const_aps.register(F32, EPS, consts.th[:, 0:1])
    b1s = A.alloc("b1s", [128, 32], F32)
    b2row = A.alloc("b2row", [1, D], F32)
    rho1 = A.alloc("rho1", [128, 1], F32)
    rho2 = A.alloc("rho2", [128, 1], F32)
    gsc = A.alloc("gsc", [128, 8], F32)

    ln_bs = A.alloc("ln_bs", [128, 48], F32)
    ln_mv = A.alloc("ln_mv", [128, 8], F32)
    ln_t = A.alloc("ln_t", [128, 4], F32)
    ln_rs = A.alloc("ln_rs", [128, 4], F32)
    ln_nb = A.alloc("ln_nb", [128, 4], F32)
    scratch_base = (A.cur + 63) & ~63
    _ln_ctr = [0]

    def ln(in_ap, in_reads, out_ap, out_writes):
        k = _ln_ctr[0] % 4
        _ln_ctr[0] += 1
        bs_ap = ln_bs.th[:, k * 12:(k + 1) * 12]
        mv_ap = ln_mv.th[:, k * 2:(k + 1) * 2]
        t_ap = ln_t.th[:, k:k + 1]
        rs_ap = ln_rs.th[:, k:k + 1]
        nb_ap = ln_nb.th[:, k:k + 1]
        bs_r = ln_bs.r(k * 48, (k + 1) * 48)
        mv_r = ln_mv.r(k * 8, (k + 1) * 8)
        t_r = ln_t.r(k * 4, (k + 1) * 4)
        rs_r = ln_rs.r(k * 4, (k + 1) * 4)
        nb_r = ln_nb.r(k * 4, (k + 1) * 4)

        def f_stats(eng, i=in_ap, o=bs_ap):
            a = eng.bn_stats(o[:, 0:6], i[:, 0:512])
            b = eng.bn_stats(o[:, 6:12], i[:, 512:1024])
            return b
        P.op("dve", f_stats, reads=in_reads, writes=[bs_r], hazard=True)

        def f_aggr(eng, i=bs_ap, o=mv_ap):
            return eng.bn_aggr(o[:, 0:2], i[:, 0:12])
        P.op("dve", f_aggr, reads=[bs_r], writes=[mv_r], hazard=True)

        def f_rstd(eng, i=mv_ap, t=t_ap, o=rs_ap):
            eng.activation(t[:, 0:1], i[:, 1:2], AF.Ln, bias=EPS, scale=1.0)
            return eng.activation(o[:, 0:1], t[:, 0:1], AF.Exp, bias=0.0,
                                  scale=-0.5)
        P.op("act", f_rstd, reads=[mv_r, consts.r(0, 4)], writes=[t_r, rs_r])

        def f_nbias(eng, i=mv_ap, rs=rs_ap, o=nb_ap):
            return eng.scalar_tensor_tensor(o[:, 0:1], i[:, 0:1], -1.0,
                                            rs[:, 0:1], ALU.mult, ALU.mult)
        P.op("dve", f_nbias, reads=[mv_r, rs_r], writes=[nb_r])

        def f_apply(eng, i=in_ap, o=out_ap, rs=rs_ap, nb=nb_ap):
            return eng.activation(o, i, AF.Identity, bias=nb[:, 0:1],
                                  scale=rs[:, 0:1])
        P.op("act", f_apply, reads=in_reads + [rs_r, nb_r], writes=out_writes)

    def f_memset(eng):
        return eng.memset(consts.th[:, 0:1], EPS)
    P.op("dve", f_memset, writes=[consts.r(0, 4)])

    def dma(queue, out_ap, in_ap, reads, writes):
        def f(e, o=out_ap, i=in_ap):
            return e.dma_start(out=o, in_=i)
        return P.op(QENG[queue], f, reads=reads, writes=writes, dq=queue)

    dma("qsp", identb.th[:], ident_d.ap()[:, :], [dr("identb", 0, 1)],
        [identb.r()])
    dma("qsp", ones_r.th[:], ones_r_d.ap()[:, :], [dr("ones_r", 0, 1)],
        [ones_r.r()])
    dma("qsp", ones_c.th[:], ones_c_d.ap()[:, :], [dr("ones_c", 0, 1)],
        [ones_c.r()])
    dma("qsp", b1s.th[:], b1m_d.ap()[:, :], [dr("b1m", 0, 1)], [b1s.r()])
    dma("qsp", b2row.th[:], b2_d.ap()[:, :], [dr("b2", 0, 1)], [b2row.r()])

    A.cur = scratch_base
    wqTb = A.alloc("wqTb", [128, 8, D], BF16)
    wkTb = A.alloc("wkTb", [128, 8, D], BF16)
    wvTb = A.alloc("wvTb", [128, 8, D], BF16)
    xtb = [A.alloc(f"xtb{i}", [128, D], F32) for i in range(4)]
    xnb = A.alloc("xnb", [128, 4, D], BF16)
    xnT = A.alloc("xnT", [128, 4, 8, 128], BF16)
    qstage = [A.alloc(f"qstage{i}", [128, 8, SPAN], BF16) for i in range(2)]
    kstage = [A.alloc(f"kstage{i}", [128, 8, SPAN], BF16) for i in range(2)]
    vstage = [A.alloc(f"vstage{i}", [128, 4, D], BF16) for i in range(1)]
    wcA = [A.alloc(f"wcA{i}", [128, 1024], F32) for i in range(4)]
    wcB = [A.alloc(f"wcB{i}", [128, 1024], F32) for i in range(4)]
    wcC = [A.alloc(f"wcC{i}", [128, 1024], F32) for i in range(1)]
    wz1 = [A.alloc(f"wz1_{i}", [128, 1024], F32) for i in range(2)]
    wt1a = A.alloc("wt1a", [128, 1024], F32)
    wz2 = [A.alloc(f"wz2_{i}", [128, 1024], F32) for i in range(2)]
    wt1b = A.alloc("wt1b", [128, 1024], F32)
    wq1b = [A.alloc(f"wq1b{i}", [128, 1024], BF16) for i in range(4)]
    wq2b = [A.alloc(f"wq2b{i}", [128, 1024], BF16) for i in range(4)]
    w1slot = A.alloc("w1slot", [128, 32], F32)
    w2slot = A.alloc("w2slot", [128, 32], F32)
    ph1_end = A.cur
    assert ph1_end <= sb_top

    XTp = psum_at(nc, "XTp", 0, [128, 1024], BF16)
    Qp = [psum_at(nc, f"Qp{i}", 1 + i, [128, 512], F32) for i in range(2)]
    Kp = [psum_at(nc, f"Kp{i}", 3 + i, [128, 512], F32) for i in range(2)]
    Vp = psum_at(nc, "Vp", 5, [128, 1024], F32)
    MBp = psum_at(nc, "MBp", 7, [128, 512], F32)

    cast_specs = []
    for dc in range(8):
        cast_specs.append((wqt_r[:, dc, :], wqTb, dc,
                           ("wqt", dc * TILB, (dc + 1) * TILB), "qact",
                           wcC[0]))
    for dc in range(8):
        cast_specs.append((wkt_r[:, dc, :], wkTb, dc,
                           ("wkt", dc * TILB, (dc + 1) * TILB), "qpool",
                           wcB[dc % 2]))
    for dc in range(8):
        q, buf = (("qact", wcC[0]) if dc < 4 else ("qpool", wcB[dc % 2]))
        cast_specs.append((wvt_r[:, dc, :], wvTb, dc,
                           ("wvt", dc * TILB, (dc + 1) * TILB), q, buf))
    for view, dst, dc, dram_r, q, buf in cast_specs:
        dma(q, buf.th[:], view, [dr(*dram_r)], [buf.r()])
        def f_cast(eng, s=buf.th, d=dst.th, dc=dc):
            return eng.activation(d[:, dc, :], s[:], AF.Identity, bias=0.0,
                                  scale=1.0)
        P.op("act", f_cast, reads=[buf.r()],
             writes=[dst.r(dc * 2048, (dc + 1) * 2048)])

    def w1_piece(i):
        c, q4 = divmod(i, 4)
        return (w1t_r[:, c, q4 * 1024:(q4 + 1) * 1024],
                ("w1t", i * TILB, (i + 1) * TILB), wcA[i % 4])

    def w2_piece(i):
        return (w2t_r[:, i, :], ("w2t", i * TILB, (i + 1) * TILB), wcB[i % 4])

    def emit_gamma_loads(b):
        if b >= 8:
            return
        for i in range(4 * b, 4 * b + 4):
            view, dram_r, buf = w1_piece(i)
            dma("qsp", buf.th[:], view, [dr(*dram_r)], [buf.r()])
            view, dram_r, buf = w2_piece(i)
            dma("qpool", buf.th[:], view, [dr(*dram_r)], [buf.r()])

    def emit_gamma_compute(b):
        for i in range(4 * b, 4 * b + 4):
            for (pc, slotb) in ((w1_piece, w1slot), (w2_piece, w2slot)):
                view, dram_r, buf = pc(i)
                def f_abs(eng, s=buf.th, d=slotb.th, si=i):
                    return eng.tensor_reduce(d[:, si:si + 1], s[:], AX.X,
                                             ALU.add,
                                             apply_absolute_value=True)
                P.op("dve", f_abs, reads=[buf.r()],
                     writes=[slotb.r(i * 4, i * 4 + 4)], hazard=True)

    emit_tern_loads = emit_gamma_loads

    def emit_tern_compute(b):
        stores = []
        for i in range(4 * b, 4 * b + 4):
            for kind in ("w1", "w2"):
                if kind == "w1":
                    view, dram_r, lb = w1_piece(i)
                    c, q4 = divmod(i, 4)
                    rho, wzb, wtb, qb = rho1, wz1[i % 2], wt1a, wq1b[i % 4]
                    oap = w1q_s.ap()[c, :, q4 * 1024:(q4 + 1) * 1024]
                    st_r = ("w1q_s", c * 128 * DFF * 2, (c + 1) * 128 * DFF * 2)
                else:
                    view, dram_r, lb = w2_piece(i)
                    rho, wzb, wtb, qb = rho2, wz2[i % 2], wt1b, wq2b[i % 4]
                    oap = w2q_s.ap()[i]
                    st_r = ("w2q_s", i * QTB, (i + 1) * QTB)
                def f_z(eng, s=lb.th, o=wzb.th, rho=rho):
                    return eng.activation(o[:], s[:], AF.Identity, bias=0.0,
                                          scale=rho.th[:, 0:1])
                P.op("act", f_z, reads=[lb.r(), rho.r()], writes=[wzb.r()])
                def f_ge(eng, wzb=wzb, wtb=wtb):
                    return eng.tensor_scalar(wtb.th[:], wzb.th[:], 0.5, None,
                                             ALU.is_ge)
                P.op("dve", f_ge, reads=[wzb.r()], writes=[wtb.r()])
                def f_le(eng, wzb=wzb):
                    return eng.tensor_scalar(wzb.th[:], wzb.th[:], -0.5, None,
                                             ALU.is_le)
                P.op("dve", f_le, reads=[wzb.r()], writes=[wzb.r()])
                def f_sub(eng, qb=qb, wzb=wzb, wtb=wtb):
                    return eng.tensor_sub(qb.th[:], wtb.th[:], wzb.th[:])
                P.op("dve", f_sub, reads=[wtb.r(), wzb.r()], writes=[qb.r()])
                stores.append((oap, qb, st_r))
        for oap, qb, st_r in stores:
            dma("qact", oap, qb.th[:], [qb.r()], [dr(*st_r)])

    def emit_gamma_final():
        for slotb, rho, gi in ((w1slot, rho1, 0), (w2slot, rho2, 1)):
            def f_sl(eng, s=slotb.th, gi=gi):
                return eng.tensor_reduce(gsc.th[:, gi:gi + 1], s[:, 0:32],
                                         AX.X, ALU.add)
            P.op("dve", f_sl, reads=[slotb.r()],
                 writes=[gsc.r(gi * 4, gi * 4 + 4)], hazard=True)
            def f_tot(eng, gi=gi):
                return eng.matmul(MBp.th[0:1, 0:1], gsc.th[:, gi:gi + 1],
                                  ones_c.th[:, :], start=True, stop=True)
            P.op("pe", f_tot, reads=[gsc.r(gi * 4, gi * 4 + 4), ones_c.r()],
                 writes=[MBp.r(0, 4)])
            def f_g2(eng, gi=gi):
                return eng.tensor_scalar(gsc.th[0:1, gi + 2:gi + 3],
                                         MBp.th[0:1, 0:1], 1.0 / NWELEM, EPS,
                                         ALU.mult, ALU.add)
            P.op("dve", f_g2, reads=[MBp.r(0, 4)],
                 writes=[gsc.r((gi + 2) * 4, (gi + 2) * 4 + 4)])
            def f_rcp(eng, gi=gi):
                return eng.reciprocal(gsc.th[0:1, gi + 4:gi + 5],
                                      gsc.th[0:1, gi + 2:gi + 3])
            P.op("dve", f_rcp, reads=[gsc.r((gi + 2) * 4, (gi + 2) * 4 + 4)],
                 writes=[gsc.r((gi + 4) * 4, (gi + 4) * 4 + 4)], hazard=True)
            def f_bc(eng, gi=gi):
                return eng.matmul(MBp.th[:, 0:1], ones_r.th[:, :],
                                  gsc.th[0:1, gi + 4:gi + 5],
                                  start=True, stop=True)
            P.op("pe", f_bc, reads=[gsc.r((gi + 4) * 4, (gi + 4) * 4 + 4),
                                    ones_r.r()], writes=[MBp.r(0, 4)])
            def f_bcc(eng, rho=rho):
                return eng.tensor_copy(rho.th[:], MBp.th[:, 0:1])
            P.op("dve", f_bcc, reads=[MBp.r(0, 4)], writes=[rho.r()])

    def emit_tern_pieces(n):
        for _ in range(n):
            if tp_i[0] >= len(tpieces):
                return
            kind, i, c, q4 = tpieces[tp_i[0]]
            tp_i[0] += 1
            if kind == "w1":
                view = w1t_r[:, c, q4 * 1024:(q4 + 1) * 1024]
                dram_r = ("w1t", i * TILB, (i + 1) * TILB)
                rho, queue, lb = rho1, "qsp", wcA[i % 2]
                wzb, wtb, qb = wz1, wt1a, wq1b[i % 2]
                oap = w1q_s.ap()[c, :, q4 * 1024:(q4 + 1) * 1024]
                st_r = ("w1q_s", c * 128 * DFF * 2, (c + 1) * 128 * DFF * 2)
            else:
                view = w2t_r[:, c, :]
                dram_r = ("w2t", c * TILB, (c + 1) * TILB)
                rho, queue, lb = rho2, "qpool", wcB[i % 2]
                wzb, wtb, qb = wz2, wt1b, wq2b[i % 2]
                oap = w2q_s.ap()[c]
                st_r = ("w2q_s", c * QTB, (c + 1) * QTB)
            dma(queue, lb.th[:], view, [dr(*dram_r)], [lb.r()])
            def f_z(eng, s=lb.th, o=wzb.th, rho=rho):
                return eng.activation(o[:], s[:], AF.Identity, bias=0.0,
                                      scale=rho.th[:, 0:1])
            P.op("act", f_z, reads=[lb.r(), rho.r()], writes=[wzb.r()])
            def f_ge(eng, wzb=wzb, wtb=wtb):
                return eng.tensor_scalar(wtb.th[:], wzb.th[:], 0.5, None,
                                         ALU.is_ge)
            P.op("dve", f_ge, reads=[wzb.r()], writes=[wtb.r()])
            def f_le(eng, wzb=wzb):
                return eng.tensor_scalar(wzb.th[:], wzb.th[:], -0.5, None,
                                         ALU.is_le)
            P.op("dve", f_le, reads=[wzb.r()], writes=[wzb.r()])
            def f_sub(eng, qb=qb, wzb=wzb, wtb=wtb):
                return eng.tensor_sub(qb.th[:], wtb.th[:], wzb.th[:])
            P.op("dve", f_sub, reads=[wtb.r(), wzb.r()], writes=[qb.r()])
            dma("qact", oap, qb.th[:], [qb.r()], [dr(*st_r)])

    for s in range(NSPAN):
        qstg = qstage[s % 2]
        kstg = kstage[s % 2]
        vstg = vstage[0]
        for j in range(4):
            ti = 4 * s + j
            buf = xtb[j]
            dma("qsp", buf.th[:], x_r[:, ti, :],
                [dr("x", ti * TILB, (ti + 1) * TILB)], [buf.r()])
        for j in range(4):
            buf = xtb[j]
            ln(buf.th[:], [buf.r(), xtb[3].r()], xnb.th[:, j, :],
               [xnb.r(j * 2048, (j + 1) * 2048)])
            def f_tp(eng, j=j):
                r = None
                for c in range(8):
                    r = eng.transpose(XTp.th[:, c * 128:(c + 1) * 128],
                                      xnb.th[:, j, c * 128:(c + 1) * 128],
                                      identb.th[:])
                return r
            P.op("pe", f_tp, reads=[xnb.r(j * 2048, (j + 1) * 2048),
                                    identb.r()], writes=[XTp.r()])
            def f_ev(eng, j=j):
                return eng.tensor_copy(xnT.th[:, j, :, :].rearrange(
                    "p a b -> p (a b)"), XTp.th[:])
            P.op("dve", f_ev, reads=[XTp.r()],
                 writes=[xnT.r(j * 2048, (j + 1) * 2048)])
        for (wb, stg, pp, scale, dst, dstnm, stq) in (
                (wqTb, qstg, Qp, INV32, qt_s, "qt_s", "qact"),
                (wkTb, kstg, Kp, 1.0, kt_s, "kt_s", "qpool")):
            for oc in range(8):
                pb = pp[oc % 2]
                def f_mm(eng, wb=wb, oc=oc, pb=pb):
                    r = None
                    for dc in range(8):
                        r = eng.matmul(pb.th[:], wb.th[:, dc, oc * 128:(oc + 1) * 128],
                                       xnT.th[:, :, dc, :],
                                       start=(dc == 0), stop=(dc == 7))
                    return r
                P.op("pe", f_mm, reads=[wb.r(), xnT.r()], writes=[pb.r()])
                def f_evq(eng, oc=oc, stg=stg, pb=pb, scale=scale):
                    return eng.activation(stg.th[:, oc, :], pb.th[:],
                                          AF.Identity, bias=0.0, scale=scale)
                P.op("act", f_evq, reads=[pb.r()],
                     writes=[stg.r(oc * 1024, (oc + 1) * 1024)])
            dma(stq, dst.ap()[s], stg.th[:],
                [stg.r()], [dr(dstnm, s * KSPANB, (s + 1) * KSPANB)])
        if s == 0:
            emit_gamma_loads(0)
        elif s < 8:
            emit_gamma_compute(s - 1)
            emit_gamma_loads(s)
        elif s == 8:
            emit_gamma_compute(7)
            emit_gamma_final()
            emit_tern_loads(0)
        else:
            emit_tern_compute(s - 9)
            emit_tern_loads(s - 8)
        for tc in range(4):
            def f_vmm(eng, tc=tc):
                r = None
                for h in range(2):
                    for dc in range(8):
                        r = eng.matmul(Vp.th[:, h * 512:(h + 1) * 512],
                                       xnT.th[:, tc, dc, :],
                                       wvTb.th[:, dc, h * 512:(h + 1) * 512],
                                       start=(dc == 0), stop=(dc == 7))
                return r
            P.op("pe", f_vmm, reads=[xnT.r(tc * 2048, (tc + 1) * 2048),
                                     wvTb.r()], writes=[Vp.r()])
            def f_vev(eng, tc=tc, vstg=vstg):
                return eng.activation(vstg.th[:, tc, :], Vp.th[:],
                                      AF.Identity, bias=0.0, scale=1.0)
            P.op("act", f_vev, reads=[Vp.r()],
                 writes=[vstg.r(tc * 2048, (tc + 1) * 2048)])
        dma("qact", v_s.ap()[:, 4 * s:4 * s + 4, :], vstg.th[:],
            [vstg.r()], [dr("v_s", s * 4 * TILB2, (s + 1) * 4 * TILB2)])
    emit_tern_compute(7)

    A.cur = scratch_base
    qbuf = [A.alloc(f"qbuf{i}", [128, 8, SPAN], BF16) for i in range(2)]
    kbuf = [A.alloc(f"kbuf{i}", [128, 8, SPAN], BF16) for i in range(5)]
    vbuf = [A.alloc(f"vbuf{i}", [128, 4, D], BF16) for i in range(4)]
    pt = [A.alloc(f"pt{i}", [128, T], BF16) for i in range(G)]
    osb = [A.alloc(f"osb{i}", [128, D], F32) for i in range(G)]
    ptsb = [A.alloc(f"ptsb{i}", [128, 512], BF16) for i in range(2)]
    diag = [A.alloc(f"diag{i}", [128, 128], BF16) for i in range(4)]
    xres = [A.alloc(f"xres{i}", [128, D], F32) for i in range(G)]
    x1t = [A.alloc(f"x1t{i}", [128, D], F32) for i in range(2)]
    nbuf = [A.alloc(f"nbuf{i}", [128, 16], F32) for i in range(G)]
    bsum = [A.alloc(f"bsum{i}", [128, 16], F32) for i in range(G)]
    cbuf = [A.alloc(f"cbuf{i}", [128, 16], F32) for i in range(G)]
    ltb = [A.alloc(f"ltb{i}", [128, 16], F32) for i in range(G)]
    mnb = A.alloc("mnb", [128, 4], F32)
    lsb = A.alloc("lsb", [128, 4], F32)
    rlb = A.alloc("rlb", [128, 4], F32)
    att_end = A.cur
    assert att_end <= sb_top

    Sp = [psum_at(nc, f"Sp{i}", i, [128, 512], F32) for i in range(3)]
    AVp = [psum_at(nc, f"AVp{i}", 3 + 2 * i, [128, 1024], F32) for i in range(2)]
    PTp0 = psum_at(nc, "PTp0", 7, [128, 512], F32)

    def load_qspan(gi):
        b = qbuf[gi % 2]
        dma("qsp", b.th[:], qt_s.ap()[gi],
            [dr("qt_s", gi * KSPANB, (gi + 1) * KSPANB)], [b.r()])

    load_qspan(0)
    pending_epi = [None]
    for g in range(NG):
        qb = qbuf[g % 2]
        if g + 1 < NG:
            load_qspan(g + 1)
        def load_kblk(j):
            dma(("qsp", "qpool")[j % 2], kbuf[j % 5].th[:],
                kt_s.ap()[j], [dr("kt_s", j * KSPANB, (j + 1) * KSPANB)],
                [kbuf[j % 5].r()])

        for j in range(4):
            load_kblk(j)
        for j in range(NBLK):
            if j == 4:
                if pending_epi[0] is not None:
                    pending_epi[0]()
                    pending_epi[0] = None
                for xi in range(G):
                    ti = G * g + xi
                    dma("qact", xres[xi].th[:], x_r[:, ti, :],
                        [dr("x", ti * TILB, (ti + 1) * TILB)], [xres[xi].r()])
            if j + 4 < NBLK:
                load_kblk(j + 4)
            kb = kbuf[j % 5]
            for xi in range(G):
                sp_ = Sp[(j * G + xi) % 3]
                def f_qk(eng, qb=qb, kb=kb, xi=xi, sp_=sp_):
                    r = None
                    for dc in range(8):
                        r = eng.matmul(sp_.th[:],
                                       qb.th[:, dc, xi * 128:(xi + 1) * 128],
                                       kb.th[:, dc, :],
                                       start=(dc == 0), stop=(dc == 7))
                    return r
                P.op("pe", f_qk, reads=[qb.r(), kb.r()], writes=[sp_.r()])
                def f_max(eng, sp_=sp_, xi=xi, j=j):
                    return eng.reduce_max(nbuf[xi].th[:, j:j + 1], sp_.th[:],
                                          AX.X, negate=True)
                P.op("dve", f_max, reads=[sp_.r()],
                     writes=[nbuf[xi].r(j * 4, (j + 1) * 4)], hazard=True)
                def f_exp(eng, sp_=sp_, xi=xi, j=j):
                    return eng.activation(
                        pt[xi].th[:, j * 512:(j + 1) * 512], sp_.th[:],
                        AF.Exp, bias=nbuf[xi].th[:, j:j + 1], scale=1.0,
                        accum_out=bsum[xi].th[:, j:j + 1])
                P.op("act", f_exp,
                     reads=[sp_.r(), nbuf[xi].r(j * 4, (j + 1) * 4)],
                     writes=[pt[xi].r(j * 1024, (j + 1) * 1024),
                             bsum[xi].r(j * 4, (j + 1) * 4)])
        for xi in range(G):
            def f_min(eng, xi=xi):
                return eng.tensor_reduce(mnb.th[:, xi:xi + 1], nbuf[xi].th[:],
                                         AX.X, ALU.min)
            P.op("dve", f_min, reads=[nbuf[xi].r()],
                 writes=[mnb.r(xi * 4, (xi + 1) * 4)], hazard=True)
        for xi in range(G):
            def f_c(eng, xi=xi):
                return eng.activation(cbuf[xi].th[:], nbuf[xi].th[:], AF.Exp,
                                      bias=mnb.th[:, xi:xi + 1], scale=-1.0)
            P.op("act", f_c, reads=[nbuf[xi].r(), mnb.r(xi * 4, (xi + 1) * 4)],
                 writes=[cbuf[xi].r()])

        def emit_denoms():
            for xi in range(G):
                def f_lt(eng, xi=xi):
                    return eng.tensor_mul(ltb[xi].th[:], bsum[xi].th[:],
                                          cbuf[xi].th[:])
                P.op("dve", f_lt, reads=[bsum[xi].r(), cbuf[xi].r()],
                     writes=[ltb[xi].r()])
                def f_ls(eng, xi=xi):
                    return eng.tensor_reduce(lsb.th[:, xi:xi + 1],
                                             ltb[xi].th[:], AX.X, ALU.add)
                P.op("dve", f_ls, reads=[ltb[xi].r()],
                     writes=[lsb.r(xi * 4, (xi + 1) * 4)], hazard=True)
                def f_rl(eng, xi=xi):
                    return eng.reciprocal(rlb.th[:, xi:xi + 1],
                                          lsb.th[:, xi:xi + 1])
                P.op("dve", f_rl, reads=[lsb.r(xi * 4, (xi + 1) * 4)],
                     writes=[rlb.r(xi * 4, (xi + 1) * 4)], hazard=True)

        def load_vblk(j):
            dma(("qact", "qsp", "qpool")[j % 3], vbuf[j % 4].th[:],
                v_s.ap()[:, 4 * j:4 * j + 4, :],
                [dr("v_s", j * 4 * TILB2, (j + 1) * 4 * TILB2)],
                [vbuf[j % 4].r()])

        for j in range(4):
            load_vblk(j)
        NSTEP = NBLK * G

        def emit_dg(s):
            j, xi = divmod(s, G)
            dg = diag[s % 4]
            def f_dg(eng, dg=dg, xi=xi, j=j):
                return eng.tensor_scalar(dg.th[:], identb.th[:],
                                         cbuf[xi].th[:, j:j + 1], None,
                                         ALU.mult)
            P.op("dve", f_dg,
                 reads=[identb.r(), cbuf[xi].r(j * 4, (j + 1) * 4)],
                 writes=[dg.r()])

        def emit_ptt(s):
            j, xi = divmod(s, G)
            dg = diag[s % 4]
            pb = PTp0
            def f_tp(eng, dg=dg, xi=xi, j=j, pb=pb):
                r = None
                for c in range(4):
                    r = eng.matmul(
                        pb.th[:, c * 128:(c + 1) * 128],
                        pt[xi].th[:, j * 512 + c * 128:j * 512 + (c + 1) * 128],
                        dg.th[:], start=True, stop=True)
                return r
            P.op("pe", f_tp,
                 reads=[dg.r(), pt[xi].r(j * 1024, (j + 1) * 1024)],
                 writes=[pb.r()])
            def f_ptev(eng, pb=pb, s=s):
                return eng.activation(ptsb[s % 2].th[:], pb.th[:],
                                      AF.Identity, bias=0.0, scale=1.0)
            P.op("act", f_ptev, reads=[pb.r()], writes=[ptsb[s % 2].r()])

        def emit_av(s):
            j, xi = divmod(s, G)
            vb = vbuf[j % 4]
            ap_ = AVp[s % 2]
            def f_av(eng, vb=vb, s=s, ap_=ap_):
                r = None
                for c in range(4):
                    for h in range(2):
                        r = eng.matmul(
                            ap_.th[:, h * 512:(h + 1) * 512],
                            ptsb[s % 2].th[:, c * 128:(c + 1) * 128],
                            vb.th[:, c, h * 512:(h + 1) * 512],
                            start=(c == 0), stop=(c == 3))
                return r
            P.op("pe", f_av, reads=[ptsb[s % 2].r(), vb.r()],
                 writes=[ap_.r()])
            if j == 0:
                def f_acc(eng, xi=xi, ap_=ap_):
                    return eng.tensor_copy(osb[xi].th[:], ap_.th[:])
                P.op("dve", f_acc, reads=[ap_.r()], writes=[osb[xi].r()])
            else:
                def f_acc(eng, xi=xi, ap_=ap_):
                    return eng.tensor_add(osb[xi].th[:], osb[xi].th[:],
                                          ap_.th[:])
                P.op("dve", f_acc, reads=[ap_.r(), osb[xi].r()],
                     writes=[osb[xi].r()])

        for s0 in range(3):
            emit_dg(s0)
        emit_ptt(0)
        for s in range(NSTEP):
            if s + 3 < NSTEP:
                emit_dg(s + 3)
            if s + 1 < NSTEP:
                emit_ptt(s + 1)
            emit_av(s)
            if s == G - 1:
                emit_denoms()
            j, xi = divmod(s, G)
            if xi == G - 1 and j + 4 < NBLK:
                load_vblk(j + 4)
        def emit_epilogue(g=g):
            for xi in range(G):
                ti = G * g + xi
                def f_aev(eng, xi=xi):
                    return eng.activation(osb[xi].th[:], osb[xi].th[:],
                                          AF.Identity, bias=0.0,
                                          scale=rlb.th[:, xi:xi + 1])
                P.op("act", f_aev,
                     reads=[osb[xi].r(), rlb.r(xi * 4, (xi + 1) * 4)],
                     writes=[osb[xi].r()])
                ln(osb[xi].th[:], [osb[xi].r()], osb[xi].th[:],
                   [osb[xi].r()])
                xt = x1t[xi % 2]
                def f_add(eng, xi=xi, xt=xt):
                    return eng.tensor_add(xt.th[:], osb[xi].th[:],
                                          xres[xi].th[:])
                P.op("dve", f_add, reads=[osb[xi].r(), xres[xi].r()],
                     writes=[xt.r()])
                dma("qact", x1_r[:, ti, :], xt.th[:], [xt.r()],
                    [dr("x1_s", ti * TILB, (ti + 1) * TILB)])
        pending_epi[0] = emit_epilogue
    pending_epi[0]()

    A.cur = scratch_base
    w1qt = A.alloc("w1qt", [128, 8, DFF], BF16)
    w2qt = A.alloc("w2qt", [128, 32, D], BF16)
    x1sp = [A.alloc(f"x1sp{i}", [128, 2, D], F32) for i in range(2)]
    xn1 = A.alloc("xn1", [128, 2, D], BF16)
    xn1T = A.alloc("xn1T", [128, 2, 8, 128], BF16)
    hb = A.alloc("hb", [128, 32, FSPAN], BF16)
    ftmp = [A.alloc(f"ftmp{i}", [128, D], F32) for i in range(2)]
    outt = [A.alloc(f"outt{i}", [128, D], F32) for i in range(2)]
    b2bc = A.alloc("b2bc", [128, D], F32)
    ffn_end = A.cur
    assert ffn_end <= sb_top

    XT1p = psum_at(nc, "XT1p", 0, [128, 1024], BF16)
    Hp = [psum_at(nc, f"Hp{i}", 1 + i, [128, 512], F32) for i in range(2)]
    Outp = [psum_at(nc, f"Outp{i}", 3 + 2 * i, [128, 1024], F32) for i in range(2)]
    MB3p = psum_at(nc, "MB3p", 7, [128, 512], F32)

    for h in range(2):
        def f_bb(eng, h=h):
            return eng.matmul(MB3p.th[:, 0:512], ones_r.th[:, :],
                              b2row.th[0:1, h * 512:(h + 1) * 512],
                              start=True, stop=True)
        P.op("pe", f_bb, reads=[ones_r.r(), b2row.r(h * 2048, (h + 1) * 2048)],
             writes=[MB3p.r()])
        def f_bbc(eng, h=h):
            return eng.tensor_copy(b2bc.th[:, h * 512:(h + 1) * 512],
                                   MB3p.th[:, 0:512])
        P.op("dve", f_bbc, reads=[MB3p.r()],
             writes=[b2bc.r(h * 2048, (h + 1) * 2048)])

    for dc in range(8):
        dma("qsp" if dc % 2 == 0 else "qpool", w1qt.th[:, dc, :], w1q_s.ap()[dc],
            [dr("w1q_s", dc * 128 * DFF * 2, (dc + 1) * 128 * DFF * 2)],
            [w1qt.r(dc * DFF * 2, (dc + 1) * DFF * 2)])
    for c8 in range(8):
        dma("qpool" if c8 % 2 == 0 else "qsp", w2qt.th[:, c8 * 4:(c8 + 1) * 4, :],
            w2q_r[:, c8 * 4:(c8 + 1) * 4, :],
            [dr("w2q_s", c8 * 4 * QTB, (c8 + 1) * 4 * QTB)],
            [w2qt.r(c8 * 4 * 2048, (c8 + 1) * 4 * 2048)])

    def load_x1span(u):
        xb = x1sp[u % 2]
        dma("qact", xb.th[:], x1_r[:, 2 * u:2 * u + 2, :],
            [dr("x1_s", 2 * u * TILB, (2 * u + 2) * TILB)], [xb.r()])

    def emit_ln3(u):
        xb = x1sp[u % 2]
        for j in range(2):
            ln(xb.th[:, j, :], [xb.r(j * 4096, (j + 1) * 4096)],
               xn1.th[:, j, :], [xn1.r(j * 2048, (j + 1) * 2048)])
            def f_tp(eng, j=j):
                r = None
                for c in range(8):
                    r = eng.transpose(XT1p.th[:, c * 128:(c + 1) * 128],
                                      xn1.th[:, j, c * 128:(c + 1) * 128],
                                      identb.th[:])
                return r
            P.op("pe", f_tp, reads=[xn1.r(j * 2048, (j + 1) * 2048),
                                    identb.r()], writes=[XT1p.r()])
            def f_ev(eng, j=j):
                return eng.tensor_copy(xn1T.th[:, j, :, :].rearrange(
                    "p a b -> p (a b)"), XT1p.th[:])
            P.op("dve", f_ev, reads=[XT1p.r()],
                 writes=[xn1T.r(j * 2048, (j + 1) * 2048)])

    load_x1span(0)
    emit_ln3(0)
    for u in range(NFSPAN):
        xb = x1sp[u % 2]
        if u + 1 < NFSPAN:
            load_x1span(u + 1)
        for fc in range(32):
            hp = Hp[fc % 2]
            def f_h(eng, fc=fc, hp=hp):
                r = None
                for dc in range(8):
                    r = eng.matmul(hp.th[:, 0:FSPAN],
                                   w1qt.th[:, dc, fc * 128:(fc + 1) * 128],
                                   xn1T.th[:, :, dc, :],
                                   start=(dc == 0), stop=(dc == 7))
                return r
            P.op("pe", f_h, reads=[w1qt.r(), xn1T.r()], writes=[hp.r()])
            def f_hev(eng, fc=fc, hp=hp):
                return eng.activation(hb.th[:, fc, :], hp.th[:, 0:FSPAN],
                                      AF.Relu, bias=b1s.th[:, fc:fc + 1],
                                      scale=1.0)
            P.op("act", f_hev, reads=[hp.r(), b1s.r(fc * 4, (fc + 1) * 4)],
                 writes=[hb.r(fc * FSPAN * 2, (fc + 1) * FSPAN * 2)])
        if u + 1 < NFSPAN:
            emit_ln3(u + 1)
        for tc in range(2):
            ti = 2 * u + tc
            op_ = Outp[tc % 2]
            def f_o(eng, tc=tc, op_=op_):
                r = None
                for fc in range(32):
                    for h in range(2):
                        r = eng.matmul(op_.th[:, h * 512:(h + 1) * 512],
                                       hb.th[:, fc, tc * 128:(tc + 1) * 128],
                                       w2qt.th[:, fc, h * 512:(h + 1) * 512],
                                       start=(fc == 0), stop=(fc == 31))
                return r
            P.op("pe", f_o, reads=[hb.r(), w2qt.r()], writes=[op_.r()])
            ft = ftmp[tc % 2]
            def f_b2(eng, op_=op_, ft=ft):
                return eng.tensor_add(ft.th[:], op_.th[:], b2bc.th[:])
            P.op("dve", f_b2, reads=[op_.r(), b2bc.r()], writes=[ft.r()])
            ln(ft.th[:], [ft.r()], ft.th[:], [ft.r()])
            ot = outt[tc % 2]
            def f_add(eng, ft=ft, ot=ot, xb=xb, tc=tc):
                return eng.tensor_add(ot.th[:], ft.th[:], xb.th[:, tc, :])
            P.op("dve", f_add,
                 reads=[ft.r(), xb.r(tc * 4096, (tc + 1) * 4096)],
                 writes=[ot.r()])
            dma("qpool", out_r[:, ti, :], ot.th[:], [ot.r()],
                [dr("out", ti * TILB, (ti + 1) * TILB)])

    def f_fence(eng):
        return eng.nop()
    P.op("pool", f_fence, reads=[dr("out", 0, T * D * 4)])

    P.finalize()
    P.emit(nc)
    return nc


_NC_CACHE = {}


def _get_nc():
    if "nc" not in _NC_CACHE:
        _NC_CACHE["nc"] = build()
    return _NC_CACHE["nc"]


def kernel(L0_attn_x, L0_attn_Wq, L0_attn_Wk, L0_attn_Wv,
           L0_ffn_W1, L0_ffn_b1, L0_ffn_W2, L0_ffn_b2):
    x = np.ascontiguousarray(np.asarray(L0_attn_x, dtype=np.float32))
    wqt = np.ascontiguousarray(np.asarray(L0_attn_Wq, dtype=np.float32).T)
    wkt = np.ascontiguousarray(np.asarray(L0_attn_Wk, dtype=np.float32).T)
    wvt = np.ascontiguousarray(np.asarray(L0_attn_Wv, dtype=np.float32).T)
    w1t = np.ascontiguousarray(np.asarray(L0_ffn_W1, dtype=np.float32).T)
    w2t = np.ascontiguousarray(np.asarray(L0_ffn_W2, dtype=np.float32).T)
    b1m = np.ascontiguousarray(
        np.asarray(L0_ffn_b1, dtype=np.float32).reshape(32, 128).T)
    b2 = np.ascontiguousarray(
        np.asarray(L0_ffn_b2, dtype=np.float32).reshape(1, D))
    nc = _get_nc()
    in_maps = []
    for c in range(8):
        in_maps.append({"x": np.ascontiguousarray(x[c]), "wqt": wqt,
                        "wkt": wkt, "wvt": wvt, "w1t": w1t, "w2t": w2t,
                        "b1m": b1m, "b2": b2})
    res = run_bass_kernel_spmd(nc, in_maps, core_ids=list(range(8)))
    return np.stack([r["out"] for r in res.results], axis=0)


# version 15
# speedup vs baseline: 1.1902x; 1.1902x over previous
import numpy as np
import ml_dtypes
import concourse.bass as bass
import concourse.mybir as mybir
from concourse.bass_utils import run_bass_kernel_spmd

F32 = mybir.dt.float32
BF16 = mybir.dt.bfloat16
AF = mybir.ActivationFunctionType
ALU = mybir.AluOpType
AX = mybir.AxisListType

T, D, DFF = 8192, 1024, 4096
NSPAN, SPAN = 16, 512
NTILE = T // 128
NBLK = 16
G = 4
NG = NTILE // G
FSPAN, NFSPAN = 256, 32
INV32 = 1.0 / 32.0
EPS = 1e-5
NWELEM = float(DFF * D)

ENGINES = ("pe", "act", "dve", "pool", "sp")
QUEUES = ("qsp", "qpool", "qact")
QENG = {"qsp": "sp", "qpool": "pool", "qact": "act"}


class _Op:
    __slots__ = ("engine", "fn", "hazard", "dq", "deps", "marked", "inc_idx",
                 "dma_idx", "waits", "seq")

    def __init__(self, engine, fn, hazard, dq, seq):
        self.engine = engine
        self.fn = fn
        self.hazard = hazard
        self.dq = dq
        self.deps = set()
        self.marked = False
        self.inc_idx = None
        self.dma_idx = None
        self.waits = []
        self.seq = seq


class Prog:

    def __init__(self):
        self.ops = []
        self.spaces = {}

    def _segs(self, sp):
        return self.spaces.setdefault(sp, [])

    def op(self, engine, fn, reads=(), writes=(), hazard=False, dq=None):
        if dq is None and engine in ("dve", "pool"):
            hazard = True
        o = _Op(engine, fn, hazard, dq, len(self.ops))
        deps = o.deps
        for (sp, lo, hi) in reads:
            segs = self._segs(sp)
            for s in segs:
                if s[0] < hi and lo < s[1]:
                    if s[2] is not None:
                        deps.add(s[2])
                    s[3].append(o)
        for (sp, lo, hi) in writes:
            segs = self._segs(sp)
            out = []
            for s in segs:
                if s[0] < hi and lo < s[1]:
                    if s[2] is not None:
                        deps.add(s[2])
                    deps.update(s[3])
                    if s[0] < lo:
                        out.append([s[0], lo, s[2], list(s[3])])
                    if hi < s[1]:
                        out.append([hi, s[1], s[2], list(s[3])])
                else:
                    out.append(s)
            out.append([lo, hi, o, []])
            self.spaces[sp] = out
        deps.discard(o)
        self.ops.append(o)
        return o

    def finalize(self):
        for o in self.ops:
            for p in o.deps:
                if p.dq is None and (p.engine != o.engine or p.hazard):
                    p.marked = True
        cnt = {e: 0 for e in ENGINES}
        dcnt = {q: 0 for q in QUEUES}
        for o in self.ops:
            if o.dq is not None:
                o.dma_idx = dcnt[o.dq]
                dcnt[o.dq] += 1
            elif o.marked:
                cnt[o.engine] += 1
                o.inc_idx = cnt[o.engine]
        watermark = {e: {} for e in ENGINES}
        for o in self.ops:
            req = {}
            for p in o.deps:
                if p.dq is not None:
                    key, val = p.dq, 16 * (p.dma_idx + 1)
                elif p.engine != o.engine or p.hazard:
                    key, val = p.engine, p.inc_idx
                else:
                    continue
                if req.get(key, 0) < val:
                    req[key] = val
            wm = watermark[o.engine]
            for k in sorted(req):
                v = req[k]
                if wm.get(k, 0) < v:
                    o.waits.append((k, v))
                    wm[k] = v

    def emit(self, nc):
        sems = {}
        for e in ENGINES:
            sems[e] = nc.alloc_semaphore("sem_" + e)
        for q in QUEUES:
            sems[q] = nc.alloc_semaphore("sem_" + q)
        by_engine = {e: [o for o in self.ops if o.engine == e] for e in ENGINES}

        def runner(ename):
            def body(eng):
                for o in by_engine[ename]:
                    for (k, v) in o.waits:
                        eng.wait_ge(sems[k], v)
                    r = o.fn(eng)
                    if isinstance(r, (list, tuple)):
                        r = r[-1]
                    if o.dq is not None:
                        r.then_inc(sems[o.dq], 16)
                    elif o.marked:
                        r.then_inc(sems[o.engine], 1)
            return body

        with nc.Block() as block:
            block.tensor(runner("pe"))
            block.scalar(runner("act"))
            block.vector(runner("dve"))
            block.gpsimd(runner("pool"))
            block.sync(runner("sp"))


class Buf:
    __slots__ = ("th", "space", "off", "size")

    def __init__(self, th, space, off, size):
        self.th = th
        self.space = space
        self.off = off
        self.size = size

    def r(self, lo=0, hi=None):
        if hi is None:
            hi = self.size
        return (self.space, self.off + lo, self.off + hi)


class SbufAlloc:
    def __init__(self, nc, base, top):
        self.nc = nc
        self.base = base
        self.top = top
        self.cur = base

    def alloc(self, name, shape, dtype):
        nbytes = int(np.prod(shape[1:])) * mybir.dt.size(dtype)
        off = (self.cur + 63) & ~63
        assert off + nbytes <= self.top, (
            f"SBUF overflow at {name}: {off}+{nbytes} > {self.top}")
        th = self.nc.alloc_sbuf_tensor_at(name, list(shape), dtype, offset=off)
        self.cur = off + nbytes
        return Buf(th, "sbuf", off, nbytes)


def psum_at(nc, name, bank, shape, dtype):
    nbytes = int(np.prod(shape[1:])) * mybir.dt.size(dtype)
    nc.psum_base = bank
    th = nc.alloc_psum_tensor(name, list(shape), dtype)
    return Buf(th, "psum", bank * 2048, nbytes)


def dr(name, lo, hi):
    return ("dram_" + name, lo, hi)


def build():
    nc = bass.Bass("TRN2", target_bir_lowering=False, debug=False)
    P = Prog()

    x_d = nc.dram_tensor("x", [T, D], F32, kind="ExternalInput")
    wqt_d = nc.dram_tensor("wqt", [D, D], F32, kind="ExternalInput")
    wkt_d = nc.dram_tensor("wkt", [D, D], F32, kind="ExternalInput")
    wvt_d = nc.dram_tensor("wvt", [D, D], F32, kind="ExternalInput")
    w1t_d = nc.dram_tensor("w1t", [D, DFF], F32, kind="ExternalInput")
    w2t_d = nc.dram_tensor("w2t", [DFF, D], F32, kind="ExternalInput")
    b1m_d = nc.dram_tensor("b1m", [128, 32], F32, kind="ExternalInput")
    b2_d = nc.dram_tensor("b2", [1, D], F32, kind="ExternalInput")
    out_d = nc.dram_tensor("out", [T, D], F32, kind="ExternalOutput")

    qt_s = nc.dram_tensor("qt_s", [NSPAN, 128, 8, SPAN], BF16, kind="Internal")
    kt_s = nc.dram_tensor("kt_s", [NSPAN, 128, 8, SPAN], BF16, kind="Internal")
    v_s = nc.dram_tensor("v_s", [128, NTILE, D], BF16, kind="Internal")
    x1_s = nc.dram_tensor("x1_s", [T, D], F32, kind="Internal")
    w1q_s = nc.dram_tensor("w1q_s", [8, 128, DFF], BF16, kind="Internal")
    w2q_s = nc.dram_tensor("w2q_s", [32, 128, D], BF16, kind="Internal")

    ident_np = np.eye(128, dtype=np.float32).astype(ml_dtypes.bfloat16)
    ident_d = nc.inline_tensor(ident_np, name="identb")
    ones_r_d = nc.inline_tensor(np.ones((1, 128), dtype=np.float32), name="ones_r")
    ones_c_d = nc.inline_tensor(np.ones((128, 1), dtype=np.float32), name="ones_c")

    x_r = x_d.ap().rearrange("(n p) d -> p n d", p=128)
    x1_r = x1_s.ap().rearrange("(n p) d -> p n d", p=128)
    out_r = out_d.ap().rearrange("(n p) d -> p n d", p=128)
    wqt_r = wqt_d.ap().rearrange("(c p) o -> p c o", p=128)
    wkt_r = wkt_d.ap().rearrange("(c p) o -> p c o", p=128)
    wvt_r = wvt_d.ap().rearrange("(c p) o -> p c o", p=128)
    w1t_r = w1t_d.ap().rearrange("(c p) f -> p c f", p=128)
    w2t_r = w2t_d.ap().rearrange("(c p) d -> p c d", p=128)
    w2q_r = w2q_s.ap().rearrange("c p d -> p c d")

    TILB = 128 * D * 4
    TILB2 = 128 * D * 2
    QTB = 128 * 1024 * 2
    SPANB = 128 * SPAN * 2
    KSPANB = 8 * SPANB

    sb_base = (nc.sbuf_base + 63) & ~63
    sb_top = nc.sbuf_top
    A = SbufAlloc(nc, sb_base, sb_top)

    identb = A.alloc("identb", [128, 128], BF16)
    ones_r = A.alloc("ones_r", [1, 128], F32)
    ones_c = A.alloc("ones_c", [128, 1], F32)
    consts = A.alloc("consts", [128, 2], F32)
    nc.const_aps.register(F32, EPS, consts.th[:, 0:1])
    b1s = A.alloc("b1s", [128, 32], F32)
    b2row = A.alloc("b2row", [1, D], F32)
    rho1 = A.alloc("rho1", [128, 1], F32)
    rho2 = A.alloc("rho2", [128, 1], F32)
    gsc = A.alloc("gsc", [128, 8], F32)

    ln_bs = A.alloc("ln_bs", [128, 48], F32)
    ln_mv = A.alloc("ln_mv", [128, 8], F32)
    ln_t = A.alloc("ln_t", [128, 4], F32)
    ln_rs = A.alloc("ln_rs", [128, 4], F32)
    ln_nb = A.alloc("ln_nb", [128, 4], F32)
    scratch_base = (A.cur + 63) & ~63
    _ln_ctr = [0]

    def ln(in_ap, in_reads, out_ap, out_writes):
        k = _ln_ctr[0] % 4
        _ln_ctr[0] += 1
        bs_ap = ln_bs.th[:, k * 12:(k + 1) * 12]
        mv_ap = ln_mv.th[:, k * 2:(k + 1) * 2]
        t_ap = ln_t.th[:, k:k + 1]
        rs_ap = ln_rs.th[:, k:k + 1]
        nb_ap = ln_nb.th[:, k:k + 1]
        bs_r = ln_bs.r(k * 48, (k + 1) * 48)
        mv_r = ln_mv.r(k * 8, (k + 1) * 8)
        t_r = ln_t.r(k * 4, (k + 1) * 4)
        rs_r = ln_rs.r(k * 4, (k + 1) * 4)
        nb_r = ln_nb.r(k * 4, (k + 1) * 4)

        def f_stats(eng, i=in_ap, o=bs_ap):
            a = eng.bn_stats(o[:, 0:6], i[:, 0:512])
            b = eng.bn_stats(o[:, 6:12], i[:, 512:1024])
            return b
        P.op("dve", f_stats, reads=in_reads, writes=[bs_r], hazard=True)

        def f_aggr(eng, i=bs_ap, o=mv_ap):
            return eng.bn_aggr(o[:, 0:2], i[:, 0:12])
        P.op("dve", f_aggr, reads=[bs_r], writes=[mv_r], hazard=True)

        def f_rstd(eng, i=mv_ap, t=t_ap, o=rs_ap):
            eng.activation(t[:, 0:1], i[:, 1:2], AF.Ln, bias=EPS, scale=1.0)
            return eng.activation(o[:, 0:1], t[:, 0:1], AF.Exp, bias=0.0,
                                  scale=-0.5)
        P.op("act", f_rstd, reads=[mv_r, consts.r(0, 4)], writes=[t_r, rs_r])

        def f_nbias(eng, i=mv_ap, rs=rs_ap, o=nb_ap):
            return eng.scalar_tensor_tensor(o[:, 0:1], i[:, 0:1], -1.0,
                                            rs[:, 0:1], ALU.mult, ALU.mult)
        P.op("dve", f_nbias, reads=[mv_r, rs_r], writes=[nb_r])

        def f_apply(eng, i=in_ap, o=out_ap, rs=rs_ap, nb=nb_ap):
            return eng.activation(o, i, AF.Identity, bias=nb[:, 0:1],
                                  scale=rs[:, 0:1])
        P.op("act", f_apply, reads=in_reads + [rs_r, nb_r], writes=out_writes)

    def f_memset(eng):
        return eng.memset(consts.th[:, 0:1], EPS)
    P.op("dve", f_memset, writes=[consts.r(0, 4)])

    def dma(queue, out_ap, in_ap, reads, writes):
        def f(e, o=out_ap, i=in_ap):
            return e.dma_start(out=o, in_=i)
        return P.op(QENG[queue], f, reads=reads, writes=writes, dq=queue)

    dma("qsp", identb.th[:], ident_d.ap()[:, :], [dr("identb", 0, 1)],
        [identb.r()])
    dma("qsp", ones_r.th[:], ones_r_d.ap()[:, :], [dr("ones_r", 0, 1)],
        [ones_r.r()])
    dma("qsp", ones_c.th[:], ones_c_d.ap()[:, :], [dr("ones_c", 0, 1)],
        [ones_c.r()])
    dma("qsp", b1s.th[:], b1m_d.ap()[:, :], [dr("b1m", 0, 1)], [b1s.r()])
    dma("qsp", b2row.th[:], b2_d.ap()[:, :], [dr("b2", 0, 1)], [b2row.r()])

    A.cur = scratch_base
    wqTb = A.alloc("wqTb", [128, 8, D], BF16)
    wkTb = A.alloc("wkTb", [128, 8, D], BF16)
    wvTb = A.alloc("wvTb", [128, 8, D], BF16)
    xtb = [A.alloc(f"xtb{i}", [128, D], F32) for i in range(4)]
    xnb = A.alloc("xnb", [128, 4, D], BF16)
    xnT = A.alloc("xnT", [128, 4, 8, 128], BF16)
    qstage = [A.alloc(f"qstage{i}", [128, 8, SPAN], BF16) for i in range(2)]
    kstage = [A.alloc(f"kstage{i}", [128, 8, SPAN], BF16) for i in range(2)]
    vstage = [A.alloc(f"vstage{i}", [128, 4, D], BF16) for i in range(1)]
    wcA = [A.alloc(f"wcA{i}", [128, 1024], F32) for i in range(4)]
    wcB = [A.alloc(f"wcB{i}", [128, 1024], F32) for i in range(4)]
    wcC = [A.alloc(f"wcC{i}", [128, 1024], F32) for i in range(1)]
    wz1 = [A.alloc(f"wz1_{i}", [128, 1024], F32) for i in range(2)]
    wt1a = A.alloc("wt1a", [128, 1024], F32)
    wz2 = [A.alloc(f"wz2_{i}", [128, 1024], F32) for i in range(2)]
    wt1b = A.alloc("wt1b", [128, 1024], F32)
    wq1b = [A.alloc(f"wq1b{i}", [128, 1024], BF16) for i in range(4)]
    wq2b = [A.alloc(f"wq2b{i}", [128, 1024], BF16) for i in range(4)]
    w1slot = A.alloc("w1slot", [128, 32], F32)
    w2slot = A.alloc("w2slot", [128, 32], F32)
    ph1_end = A.cur
    assert ph1_end <= sb_top

    XTp = psum_at(nc, "XTp", 0, [128, 1024], BF16)
    Qp = [psum_at(nc, f"Qp{i}", 1 + i, [128, 512], F32) for i in range(2)]
    Kp = [psum_at(nc, f"Kp{i}", 3 + i, [128, 512], F32) for i in range(2)]
    Vp = psum_at(nc, "Vp", 5, [128, 1024], F32)
    MBp = psum_at(nc, "MBp", 7, [128, 512], F32)

    cast_specs = []
    for dc in range(8):
        cast_specs.append((wqt_r[:, dc, :], wqTb, dc,
                           ("wqt", dc * TILB, (dc + 1) * TILB), "qact",
                           wcC[0]))
    for dc in range(8):
        cast_specs.append((wkt_r[:, dc, :], wkTb, dc,
                           ("wkt", dc * TILB, (dc + 1) * TILB), "qpool",
                           wcB[dc % 2]))
    for dc in range(8):
        q, buf = (("qact", wcC[0]) if dc < 4 else ("qpool", wcB[dc % 2]))
        cast_specs.append((wvt_r[:, dc, :], wvTb, dc,
                           ("wvt", dc * TILB, (dc + 1) * TILB), q, buf))
    for view, dst, dc, dram_r, q, buf in cast_specs:
        dma(q, buf.th[:], view, [dr(*dram_r)], [buf.r()])
        def f_cast(eng, s=buf.th, d=dst.th, dc=dc):
            return eng.activation(d[:, dc, :], s[:], AF.Identity, bias=0.0,
                                  scale=1.0)
        P.op("act", f_cast, reads=[buf.r()],
             writes=[dst.r(dc * 2048, (dc + 1) * 2048)])

    def w1_piece(i):
        c, q4 = divmod(i, 4)
        return (w1t_r[:, c, q4 * 1024:(q4 + 1) * 1024],
                ("w1t", i * TILB, (i + 1) * TILB), wcA[i % 4])

    def w2_piece(i):
        return (w2t_r[:, i, :], ("w2t", i * TILB, (i + 1) * TILB), wcB[i % 4])

    def emit_gamma_loads(b):
        if b >= 8:
            return
        for i in range(4 * b, 4 * b + 4):
            view, dram_r, buf = w1_piece(i)
            dma("qsp", buf.th[:], view, [dr(*dram_r)], [buf.r()])
            view, dram_r, buf = w2_piece(i)
            dma("qpool", buf.th[:], view, [dr(*dram_r)], [buf.r()])

    def emit_gamma_compute(b):
        for i in range(4 * b, 4 * b + 4):
            for (pc, slotb) in ((w1_piece, w1slot), (w2_piece, w2slot)):
                view, dram_r, buf = pc(i)
                def f_abs(eng, s=buf.th, d=slotb.th, si=i):
                    return eng.tensor_reduce(d[:, si:si + 1], s[:], AX.X,
                                             ALU.add,
                                             apply_absolute_value=True)
                P.op("dve", f_abs, reads=[buf.r()],
                     writes=[slotb.r(i * 4, i * 4 + 4)], hazard=True)

    emit_tern_loads = emit_gamma_loads

    def emit_tern_compute(b):
        stores = []
        for i in range(4 * b, 4 * b + 4):
            for kind in ("w1", "w2"):
                if kind == "w1":
                    view, dram_r, lb = w1_piece(i)
                    c, q4 = divmod(i, 4)
                    rho, wzb, wtb, qb = rho1, wz1[i % 2], wt1a, wq1b[i % 4]
                    oap = w1q_s.ap()[c, :, q4 * 1024:(q4 + 1) * 1024]
                    st_r = ("w1q_s", c * 128 * DFF * 2, (c + 1) * 128 * DFF * 2)
                else:
                    view, dram_r, lb = w2_piece(i)
                    rho, wzb, wtb, qb = rho2, wz2[i % 2], wt1b, wq2b[i % 4]
                    oap = w2q_s.ap()[i]
                    st_r = ("w2q_s", i * QTB, (i + 1) * QTB)
                def f_z(eng, s=lb.th, o=wzb.th, rho=rho):
                    return eng.activation(o[:], s[:], AF.Identity, bias=0.0,
                                          scale=rho.th[:, 0:1])
                P.op("act", f_z, reads=[lb.r(), rho.r()], writes=[wzb.r()])
                def f_ge(eng, wzb=wzb, wtb=wtb):
                    return eng.tensor_scalar(wtb.th[:], wzb.th[:], 0.5, None,
                                             ALU.is_ge)
                P.op("dve", f_ge, reads=[wzb.r()], writes=[wtb.r()])
                def f_le(eng, wzb=wzb):
                    return eng.tensor_scalar(wzb.th[:], wzb.th[:], -0.5, None,
                                             ALU.is_le)
                P.op("dve", f_le, reads=[wzb.r()], writes=[wzb.r()])
                def f_sub(eng, qb=qb, wzb=wzb, wtb=wtb):
                    return eng.tensor_sub(qb.th[:], wtb.th[:], wzb.th[:])
                P.op("dve", f_sub, reads=[wtb.r(), wzb.r()], writes=[qb.r()])
                stores.append((oap, qb, st_r))
        for oap, qb, st_r in stores:
            dma("qact", oap, qb.th[:], [qb.r()], [dr(*st_r)])

    def emit_gamma_final():
        for slotb, rho, gi in ((w1slot, rho1, 0), (w2slot, rho2, 1)):
            def f_sl(eng, s=slotb.th, gi=gi):
                return eng.tensor_reduce(gsc.th[:, gi:gi + 1], s[:, 0:32],
                                         AX.X, ALU.add)
            P.op("dve", f_sl, reads=[slotb.r()],
                 writes=[gsc.r(gi * 4, gi * 4 + 4)], hazard=True)
            def f_tot(eng, gi=gi):
                return eng.matmul(MBp.th[0:1, 0:1], gsc.th[:, gi:gi + 1],
                                  ones_c.th[:, :], start=True, stop=True)
            P.op("pe", f_tot, reads=[gsc.r(gi * 4, gi * 4 + 4), ones_c.r()],
                 writes=[MBp.r(0, 4)])
            def f_g2(eng, gi=gi):
                return eng.tensor_scalar(gsc.th[0:1, gi + 2:gi + 3],
                                         MBp.th[0:1, 0:1], 1.0 / NWELEM, EPS,
                                         ALU.mult, ALU.add)
            P.op("dve", f_g2, reads=[MBp.r(0, 4)],
                 writes=[gsc.r((gi + 2) * 4, (gi + 2) * 4 + 4)])
            def f_rcp(eng, gi=gi):
                return eng.reciprocal(gsc.th[0:1, gi + 4:gi + 5],
                                      gsc.th[0:1, gi + 2:gi + 3])
            P.op("dve", f_rcp, reads=[gsc.r((gi + 2) * 4, (gi + 2) * 4 + 4)],
                 writes=[gsc.r((gi + 4) * 4, (gi + 4) * 4 + 4)], hazard=True)
            def f_bc(eng, gi=gi):
                return eng.matmul(MBp.th[:, 0:1], ones_r.th[:, :],
                                  gsc.th[0:1, gi + 4:gi + 5],
                                  start=True, stop=True)
            P.op("pe", f_bc, reads=[gsc.r((gi + 4) * 4, (gi + 4) * 4 + 4),
                                    ones_r.r()], writes=[MBp.r(0, 4)])
            def f_bcc(eng, rho=rho):
                return eng.tensor_copy(rho.th[:], MBp.th[:, 0:1])
            P.op("dve", f_bcc, reads=[MBp.r(0, 4)], writes=[rho.r()])

    def emit_tern_pieces(n):
        for _ in range(n):
            if tp_i[0] >= len(tpieces):
                return
            kind, i, c, q4 = tpieces[tp_i[0]]
            tp_i[0] += 1
            if kind == "w1":
                view = w1t_r[:, c, q4 * 1024:(q4 + 1) * 1024]
                dram_r = ("w1t", i * TILB, (i + 1) * TILB)
                rho, queue, lb = rho1, "qsp", wcA[i % 2]
                wzb, wtb, qb = wz1, wt1a, wq1b[i % 2]
                oap = w1q_s.ap()[c, :, q4 * 1024:(q4 + 1) * 1024]
                st_r = ("w1q_s", c * 128 * DFF * 2, (c + 1) * 128 * DFF * 2)
            else:
                view = w2t_r[:, c, :]
                dram_r = ("w2t", c * TILB, (c + 1) * TILB)
                rho, queue, lb = rho2, "qpool", wcB[i % 2]
                wzb, wtb, qb = wz2, wt1b, wq2b[i % 2]
                oap = w2q_s.ap()[c]
                st_r = ("w2q_s", c * QTB, (c + 1) * QTB)
            dma(queue, lb.th[:], view, [dr(*dram_r)], [lb.r()])
            def f_z(eng, s=lb.th, o=wzb.th, rho=rho):
                return eng.activation(o[:], s[:], AF.Identity, bias=0.0,
                                      scale=rho.th[:, 0:1])
            P.op("act", f_z, reads=[lb.r(), rho.r()], writes=[wzb.r()])
            def f_ge(eng, wzb=wzb, wtb=wtb):
                return eng.tensor_scalar(wtb.th[:], wzb.th[:], 0.5, None,
                                         ALU.is_ge)
            P.op("dve", f_ge, reads=[wzb.r()], writes=[wtb.r()])
            def f_le(eng, wzb=wzb):
                return eng.tensor_scalar(wzb.th[:], wzb.th[:], -0.5, None,
                                         ALU.is_le)
            P.op("dve", f_le, reads=[wzb.r()], writes=[wzb.r()])
            def f_sub(eng, qb=qb, wzb=wzb, wtb=wtb):
                return eng.tensor_sub(qb.th[:], wtb.th[:], wzb.th[:])
            P.op("dve", f_sub, reads=[wtb.r(), wzb.r()], writes=[qb.r()])
            dma("qact", oap, qb.th[:], [qb.r()], [dr(*st_r)])

    for s in range(NSPAN):
        qstg = qstage[s % 2]
        kstg = kstage[s % 2]
        vstg = vstage[0]
        for j in range(4):
            ti = 4 * s + j
            buf = xtb[j]
            dma("qsp", buf.th[:], x_r[:, ti, :],
                [dr("x", ti * TILB, (ti + 1) * TILB)], [buf.r()])
        for j in range(4):
            buf = xtb[j]
            ln(buf.th[:], [buf.r(), xtb[3].r()], xnb.th[:, j, :],
               [xnb.r(j * 2048, (j + 1) * 2048)])
            def f_tp(eng, j=j):
                r = None
                for c in range(8):
                    r = eng.transpose(XTp.th[:, c * 128:(c + 1) * 128],
                                      xnb.th[:, j, c * 128:(c + 1) * 128],
                                      identb.th[:])
                return r
            P.op("pe", f_tp, reads=[xnb.r(j * 2048, (j + 1) * 2048),
                                    identb.r()], writes=[XTp.r()])
            def f_ev(eng, j=j):
                return eng.tensor_copy(xnT.th[:, j, :, :].rearrange(
                    "p a b -> p (a b)"), XTp.th[:])
            P.op("dve", f_ev, reads=[XTp.r()],
                 writes=[xnT.r(j * 2048, (j + 1) * 2048)])
        for (wb, stg, pp, scale, dst, dstnm, stq) in (
                (wqTb, qstg, Qp, INV32, qt_s, "qt_s", "qact"),
                (wkTb, kstg, Kp, 1.0, kt_s, "kt_s", "qpool")):
            for oc in range(8):
                pb = pp[oc % 2]
                def f_mm(eng, wb=wb, oc=oc, pb=pb):
                    r = None
                    for dc in range(8):
                        r = eng.matmul(pb.th[:], wb.th[:, dc, oc * 128:(oc + 1) * 128],
                                       xnT.th[:, :, dc, :],
                                       start=(dc == 0), stop=(dc == 7))
                    return r
                P.op("pe", f_mm, reads=[wb.r(), xnT.r()], writes=[pb.r()])
                def f_evq(eng, oc=oc, stg=stg, pb=pb, scale=scale):
                    return eng.activation(stg.th[:, oc, :], pb.th[:],
                                          AF.Identity, bias=0.0, scale=scale)
                P.op("act", f_evq, reads=[pb.r()],
                     writes=[stg.r(oc * 1024, (oc + 1) * 1024)])
            dma(stq, dst.ap()[s], stg.th[:],
                [stg.r()], [dr(dstnm, s * KSPANB, (s + 1) * KSPANB)])
        if s == 0:
            emit_gamma_loads(0)
        elif s < 8:
            emit_gamma_compute(s - 1)
            emit_gamma_loads(s)
        elif s == 8:
            emit_gamma_compute(7)
            emit_gamma_final()
            emit_tern_loads(0)
        else:
            emit_tern_compute(s - 9)
            emit_tern_loads(s - 8)
        for tc in range(4):
            def f_vmm(eng, tc=tc):
                r = None
                for h in range(2):
                    for dc in range(8):
                        r = eng.matmul(Vp.th[:, h * 512:(h + 1) * 512],
                                       xnT.th[:, tc, dc, :],
                                       wvTb.th[:, dc, h * 512:(h + 1) * 512],
                                       start=(dc == 0), stop=(dc == 7))
                return r
            P.op("pe", f_vmm, reads=[xnT.r(tc * 2048, (tc + 1) * 2048),
                                     wvTb.r()], writes=[Vp.r()])
            def f_vev(eng, tc=tc, vstg=vstg):
                return eng.activation(vstg.th[:, tc, :], Vp.th[:],
                                      AF.Identity, bias=0.0, scale=1.0)
            P.op("act", f_vev, reads=[Vp.r()],
                 writes=[vstg.r(tc * 2048, (tc + 1) * 2048)])
        dma("qact", v_s.ap()[:, 4 * s:4 * s + 4, :], vstg.th[:],
            [vstg.r()], [dr("v_s", s * 4 * TILB2, (s + 1) * 4 * TILB2)])
    emit_tern_compute(7)

    A.cur = scratch_base
    qbuf = [A.alloc(f"qbuf{i}", [128, 8, SPAN], BF16) for i in range(2)]
    kbuf = [A.alloc(f"kbuf{i}", [128, 8, SPAN], BF16) for i in range(5)]
    vbuf = [A.alloc(f"vbuf{i}", [128, 4, D], BF16) for i in range(4)]
    pt = [A.alloc(f"pt{i}", [128, T], BF16) for i in range(G)]
    osb = [A.alloc(f"osb{i}", [128, D], F32) for i in range(G)]
    ptsb = [A.alloc(f"ptsb{i}", [128, 512], BF16) for i in range(2)]
    diag = [A.alloc(f"diag{i}", [128, 128], BF16) for i in range(4)]
    xres = [A.alloc(f"xres{i}", [128, D], F32) for i in range(G)]
    x1t = [A.alloc(f"x1t{i}", [128, D], F32) for i in range(2)]
    nbuf = [A.alloc(f"nbuf{i}", [128, 16], F32) for i in range(G)]
    bsum = [A.alloc(f"bsum{i}", [128, 16], F32) for i in range(G)]
    cbuf = [A.alloc(f"cbuf{i}", [128, 16], F32) for i in range(G)]
    ltb = [A.alloc(f"ltb{i}", [128, 16], F32) for i in range(G)]
    mnb = A.alloc("mnb", [128, 4], F32)
    lsb = A.alloc("lsb", [128, 4], F32)
    rlb = A.alloc("rlb", [128, 4], F32)
    att_end = A.cur
    assert att_end <= sb_top

    Sp = [psum_at(nc, f"Sp{i}", i, [128, 512], F32) for i in range(3)]
    AVp = [psum_at(nc, f"AVp{i}", 3 + 2 * i, [128, 1024], F32) for i in range(2)]
    PTp0 = psum_at(nc, "PTp0", 7, [128, 512], F32)

    def load_qspan(gi):
        b = qbuf[gi % 2]
        dma("qsp", b.th[:], qt_s.ap()[gi],
            [dr("qt_s", gi * KSPANB, (gi + 1) * KSPANB)], [b.r()])

    load_qspan(0)
    pending_epi = [None]
    for g in range(NG):
        qb = qbuf[g % 2]
        if g + 1 < NG:
            load_qspan(g + 1)
        def load_kblk(j):
            dma(("qsp", "qpool")[j % 2], kbuf[j % 5].th[:],
                kt_s.ap()[j], [dr("kt_s", j * KSPANB, (j + 1) * KSPANB)],
                [kbuf[j % 5].r()])

        for j in range(4):
            load_kblk(j)
        for j in range(NBLK):
            if j == 4:
                if pending_epi[0] is not None:
                    pending_epi[0]()
                    pending_epi[0] = None
                for xi in range(G):
                    ti = G * g + xi
                    dma("qact", xres[xi].th[:], x_r[:, ti, :],
                        [dr("x", ti * TILB, (ti + 1) * TILB)], [xres[xi].r()])
            if j + 4 < NBLK:
                load_kblk(j + 4)
            kb = kbuf[j % 5]
            for xi in range(G):
                sp_ = Sp[(j * G + xi) % 3]
                def f_qk(eng, qb=qb, kb=kb, xi=xi, sp_=sp_):
                    r = None
                    for dc in range(8):
                        r = eng.matmul(sp_.th[:],
                                       qb.th[:, dc, xi * 128:(xi + 1) * 128],
                                       kb.th[:, dc, :],
                                       start=(dc == 0), stop=(dc == 7))
                    return r
                P.op("pe", f_qk, reads=[qb.r(), kb.r()], writes=[sp_.r()])
                def f_max(eng, sp_=sp_, xi=xi, j=j):
                    return eng.reduce_max(nbuf[xi].th[:, j:j + 1], sp_.th[:],
                                          AX.X, negate=True)
                P.op("dve", f_max, reads=[sp_.r()],
                     writes=[nbuf[xi].r(j * 4, (j + 1) * 4)], hazard=True)
                def f_exp(eng, sp_=sp_, xi=xi, j=j):
                    return eng.activation(
                        pt[xi].th[:, j * 512:(j + 1) * 512], sp_.th[:],
                        AF.Exp, bias=nbuf[xi].th[:, j:j + 1], scale=1.0,
                        accum_out=bsum[xi].th[:, j:j + 1])
                P.op("act", f_exp,
                     reads=[sp_.r(), nbuf[xi].r(j * 4, (j + 1) * 4)],
                     writes=[pt[xi].r(j * 1024, (j + 1) * 1024),
                             bsum[xi].r(j * 4, (j + 1) * 4)])
        for xi in range(G):
            def f_min(eng, xi=xi):
                return eng.tensor_reduce(mnb.th[:, xi:xi + 1], nbuf[xi].th[:],
                                         AX.X, ALU.min)
            P.op("dve", f_min, reads=[nbuf[xi].r()],
                 writes=[mnb.r(xi * 4, (xi + 1) * 4)], hazard=True)
        for xi in range(G):
            def f_c(eng, xi=xi):
                return eng.activation(cbuf[xi].th[:], nbuf[xi].th[:], AF.Exp,
                                      bias=mnb.th[:, xi:xi + 1], scale=-1.0)
            P.op("act", f_c, reads=[nbuf[xi].r(), mnb.r(xi * 4, (xi + 1) * 4)],
                 writes=[cbuf[xi].r()])

        def emit_denoms():
            for xi in range(G):
                def f_lt(eng, xi=xi):
                    return eng.tensor_mul(ltb[xi].th[:], bsum[xi].th[:],
                                          cbuf[xi].th[:])
                P.op("dve", f_lt, reads=[bsum[xi].r(), cbuf[xi].r()],
                     writes=[ltb[xi].r()])
                def f_ls(eng, xi=xi):
                    return eng.tensor_reduce(lsb.th[:, xi:xi + 1],
                                             ltb[xi].th[:], AX.X, ALU.add)
                P.op("dve", f_ls, reads=[ltb[xi].r()],
                     writes=[lsb.r(xi * 4, (xi + 1) * 4)], hazard=True)
                def f_rl(eng, xi=xi):
                    return eng.reciprocal(rlb.th[:, xi:xi + 1],
                                          lsb.th[:, xi:xi + 1])
                P.op("dve", f_rl, reads=[lsb.r(xi * 4, (xi + 1) * 4)],
                     writes=[rlb.r(xi * 4, (xi + 1) * 4)], hazard=True)

        def load_vblk(j):
            dma(("qact", "qsp", "qpool")[j % 3], vbuf[j % 4].th[:],
                v_s.ap()[:, 4 * j:4 * j + 4, :],
                [dr("v_s", j * 4 * TILB2, (j + 1) * 4 * TILB2)],
                [vbuf[j % 4].r()])

        for j in range(4):
            load_vblk(j)
        NSTEP = NBLK * G

        def emit_dg(s):
            j, xi = divmod(s, G)
            dg = diag[s % 4]
            def f_dg(eng, dg=dg, xi=xi, j=j):
                return eng.tensor_scalar(dg.th[:], identb.th[:],
                                         cbuf[xi].th[:, j:j + 1], None,
                                         ALU.mult)
            P.op("dve", f_dg,
                 reads=[identb.r(), cbuf[xi].r(j * 4, (j + 1) * 4)],
                 writes=[dg.r()])

        def emit_ptt(s):
            j, xi = divmod(s, G)
            dg = diag[s % 4]
            pb = PTp0
            def f_tp(eng, dg=dg, xi=xi, j=j, pb=pb):
                r = None
                for c in range(4):
                    r = eng.matmul(
                        pb.th[:, c * 128:(c + 1) * 128],
                        pt[xi].th[:, j * 512 + c * 128:j * 512 + (c + 1) * 128],
                        dg.th[:], start=True, stop=True)
                return r
            P.op("pe", f_tp,
                 reads=[dg.r(), pt[xi].r(j * 1024, (j + 1) * 1024)],
                 writes=[pb.r()])
            def f_ptev(eng, pb=pb, s=s):
                return eng.activation(ptsb[s % 2].th[:], pb.th[:],
                                      AF.Identity, bias=0.0, scale=1.0)
            P.op("act", f_ptev, reads=[pb.r()], writes=[ptsb[s % 2].r()])

        def emit_av(s):
            j, xi = divmod(s, G)
            vb = vbuf[j % 4]
            ap_ = AVp[s % 2]
            def f_av(eng, vb=vb, s=s, ap_=ap_):
                r = None
                for c in range(4):
                    for h in range(2):
                        r = eng.matmul(
                            ap_.th[:, h * 512:(h + 1) * 512],
                            ptsb[s % 2].th[:, c * 128:(c + 1) * 128],
                            vb.th[:, c, h * 512:(h + 1) * 512],
                            start=(c == 0), stop=(c == 3))
                return r
            P.op("pe", f_av, reads=[ptsb[s % 2].r(), vb.r()],
                 writes=[ap_.r()])
            if j == 0:
                def f_acc(eng, xi=xi, ap_=ap_):
                    return eng.tensor_copy(osb[xi].th[:], ap_.th[:])
                P.op("dve", f_acc, reads=[ap_.r()], writes=[osb[xi].r()])
            else:
                def f_acc(eng, xi=xi, ap_=ap_):
                    return eng.tensor_add(osb[xi].th[:], osb[xi].th[:],
                                          ap_.th[:])
                P.op("dve", f_acc, reads=[ap_.r(), osb[xi].r()],
                     writes=[osb[xi].r()])

        for s0 in range(3):
            emit_dg(s0)
        emit_ptt(0)
        for s in range(NSTEP):
            if s + 3 < NSTEP:
                emit_dg(s + 3)
            if s + 1 < NSTEP:
                emit_ptt(s + 1)
            emit_av(s)
            if s == G - 1:
                emit_denoms()
            j, xi = divmod(s, G)
            if xi == G - 1 and j + 4 < NBLK:
                load_vblk(j + 4)
        def emit_epilogue(g=g):
            for xi in range(G):
                ti = G * g + xi
                def f_aev(eng, xi=xi):
                    return eng.activation(osb[xi].th[:], osb[xi].th[:],
                                          AF.Identity, bias=0.0,
                                          scale=rlb.th[:, xi:xi + 1])
                P.op("act", f_aev,
                     reads=[osb[xi].r(), rlb.r(xi * 4, (xi + 1) * 4)],
                     writes=[osb[xi].r()])
                ln(osb[xi].th[:], [osb[xi].r()], osb[xi].th[:],
                   [osb[xi].r()])
                xt = x1t[xi % 2]
                def f_add(eng, xi=xi, xt=xt):
                    return eng.tensor_add(xt.th[:], osb[xi].th[:],
                                          xres[xi].th[:])
                P.op("dve", f_add, reads=[osb[xi].r(), xres[xi].r()],
                     writes=[xt.r()])
                dma("qact", x1_r[:, ti, :], xt.th[:], [xt.r()],
                    [dr("x1_s", ti * TILB, (ti + 1) * TILB)])
        pending_epi[0] = emit_epilogue
    pending_epi[0]()

    A.cur = scratch_base
    w1qt = A.alloc("w1qt", [128, 8, DFF], BF16)
    w2qt = A.alloc("w2qt", [128, 32, D], BF16)
    x1sp = [A.alloc(f"x1sp{i}", [128, 2, D], F32) for i in range(2)]
    xn1 = A.alloc("xn1", [128, 2, D], BF16)
    xn1T = A.alloc("xn1T", [128, 2, 8, 128], BF16)
    hb = A.alloc("hb", [128, 32, FSPAN], BF16)
    ftmp = [A.alloc(f"ftmp{i}", [128, D], F32) for i in range(2)]
    outt = [A.alloc(f"outt{i}", [128, D], F32) for i in range(2)]
    b2bc = A.alloc("b2bc", [128, D], F32)
    ffn_end = A.cur
    assert ffn_end <= sb_top

    XT1p = psum_at(nc, "XT1p", 0, [128, 1024], BF16)
    Hp = [psum_at(nc, f"Hp{i}", b, [128, 512], F32)
          for i, b in enumerate((1, 2, 7))]
    Outp = [psum_at(nc, f"Outp{i}", 3 + 2 * i, [128, 1024], F32) for i in range(2)]
    MB3p = psum_at(nc, "MB3p", 7, [128, 512], F32)

    for h in range(2):
        def f_bb(eng, h=h):
            return eng.matmul(MB3p.th[:, 0:512], ones_r.th[:, :],
                              b2row.th[0:1, h * 512:(h + 1) * 512],
                              start=True, stop=True)
        P.op("pe", f_bb, reads=[ones_r.r(), b2row.r(h * 2048, (h + 1) * 2048)],
             writes=[MB3p.r()])
        def f_bbc(eng, h=h):
            return eng.tensor_copy(b2bc.th[:, h * 512:(h + 1) * 512],
                                   MB3p.th[:, 0:512])
        P.op("dve", f_bbc, reads=[MB3p.r()],
             writes=[b2bc.r(h * 2048, (h + 1) * 2048)])

    for dc in range(8):
        dma("qsp" if dc % 2 == 0 else "qpool", w1qt.th[:, dc, :], w1q_s.ap()[dc],
            [dr("w1q_s", dc * 128 * DFF * 2, (dc + 1) * 128 * DFF * 2)],
            [w1qt.r(dc * DFF * 2, (dc + 1) * DFF * 2)])
    for c8 in range(8):
        dma("qpool" if c8 % 2 == 0 else "qsp", w2qt.th[:, c8 * 4:(c8 + 1) * 4, :],
            w2q_r[:, c8 * 4:(c8 + 1) * 4, :],
            [dr("w2q_s", c8 * 4 * QTB, (c8 + 1) * 4 * QTB)],
            [w2qt.r(c8 * 4 * 2048, (c8 + 1) * 4 * 2048)])

    def load_x1span(u):
        xb = x1sp[u % 2]
        dma("qact", xb.th[:], x1_r[:, 2 * u:2 * u + 2, :],
            [dr("x1_s", 2 * u * TILB, (2 * u + 2) * TILB)], [xb.r()])

    def emit_ln3(u):
        xb = x1sp[u % 2]
        for j in range(2):
            ln(xb.th[:, j, :], [xb.r(j * 4096, (j + 1) * 4096)],
               xn1.th[:, j, :], [xn1.r(j * 2048, (j + 1) * 2048)])
            def f_tp(eng, j=j):
                r = None
                for c in range(8):
                    r = eng.transpose(XT1p.th[:, c * 128:(c + 1) * 128],
                                      xn1.th[:, j, c * 128:(c + 1) * 128],
                                      identb.th[:])
                return r
            P.op("pe", f_tp, reads=[xn1.r(j * 2048, (j + 1) * 2048),
                                    identb.r()], writes=[XT1p.r()])
            def f_ev(eng, j=j):
                return eng.tensor_copy(xn1T.th[:, j, :, :].rearrange(
                    "p a b -> p (a b)"), XT1p.th[:])
            P.op("dve", f_ev, reads=[XT1p.r()],
                 writes=[xn1T.r(j * 2048, (j + 1) * 2048)])

    load_x1span(0)
    emit_ln3(0)
    for u in range(NFSPAN):
        xb = x1sp[u % 2]
        if u + 1 < NFSPAN:
            load_x1span(u + 1)
        for fc in range(32):
            hp = Hp[fc % 3]
            def f_h(eng, fc=fc, hp=hp):
                r = None
                for dc in range(8):
                    r = eng.matmul(hp.th[:, 0:FSPAN],
                                   w1qt.th[:, dc, fc * 128:(fc + 1) * 128],
                                   xn1T.th[:, :, dc, :],
                                   start=(dc == 0), stop=(dc == 7))
                return r
            P.op("pe", f_h, reads=[w1qt.r(), xn1T.r()], writes=[hp.r()])
            def f_hev(eng, fc=fc, hp=hp):
                return eng.activation(hb.th[:, fc, :], hp.th[:, 0:FSPAN],
                                      AF.Relu, bias=b1s.th[:, fc:fc + 1],
                                      scale=1.0)
            P.op("act", f_hev, reads=[hp.r(), b1s.r(fc * 4, (fc + 1) * 4)],
                 writes=[hb.r(fc * FSPAN * 2, (fc + 1) * FSPAN * 2)])
        if u + 1 < NFSPAN:
            emit_ln3(u + 1)
        for tc in range(2):
            ti = 2 * u + tc
            op_ = Outp[tc % 2]
            def f_o(eng, tc=tc, op_=op_):
                r = None
                for fc in range(32):
                    for h in range(2):
                        r = eng.matmul(op_.th[:, h * 512:(h + 1) * 512],
                                       hb.th[:, fc, tc * 128:(tc + 1) * 128],
                                       w2qt.th[:, fc, h * 512:(h + 1) * 512],
                                       start=(fc == 0), stop=(fc == 31))
                return r
            P.op("pe", f_o, reads=[hb.r(), w2qt.r()], writes=[op_.r()])
            ft = ftmp[tc % 2]
            def f_b2(eng, op_=op_, ft=ft):
                return eng.tensor_add(ft.th[:], op_.th[:], b2bc.th[:])
            P.op("dve", f_b2, reads=[op_.r(), b2bc.r()], writes=[ft.r()])
            ln(ft.th[:], [ft.r()], ft.th[:], [ft.r()])
            ot = outt[tc % 2]
            def f_add(eng, ft=ft, ot=ot, xb=xb, tc=tc):
                return eng.tensor_add(ot.th[:], ft.th[:], xb.th[:, tc, :])
            P.op("dve", f_add,
                 reads=[ft.r(), xb.r(tc * 4096, (tc + 1) * 4096)],
                 writes=[ot.r()])
            dma("qpool", out_r[:, ti, :], ot.th[:], [ot.r()],
                [dr("out", ti * TILB, (ti + 1) * TILB)])

    def f_fence(eng):
        return eng.nop()
    P.op("pool", f_fence, reads=[dr("out", 0, T * D * 4)])

    P.finalize()
    P.emit(nc)
    return nc


_NC_CACHE = {}


def _get_nc():
    if "nc" not in _NC_CACHE:
        _NC_CACHE["nc"] = build()
    return _NC_CACHE["nc"]


def kernel(L0_attn_x, L0_attn_Wq, L0_attn_Wk, L0_attn_Wv,
           L0_ffn_W1, L0_ffn_b1, L0_ffn_W2, L0_ffn_b2):
    x = np.ascontiguousarray(np.asarray(L0_attn_x, dtype=np.float32))
    wqt = np.ascontiguousarray(np.asarray(L0_attn_Wq, dtype=np.float32).T)
    wkt = np.ascontiguousarray(np.asarray(L0_attn_Wk, dtype=np.float32).T)
    wvt = np.ascontiguousarray(np.asarray(L0_attn_Wv, dtype=np.float32).T)
    w1t = np.ascontiguousarray(np.asarray(L0_ffn_W1, dtype=np.float32).T)
    w2t = np.ascontiguousarray(np.asarray(L0_ffn_W2, dtype=np.float32).T)
    b1m = np.ascontiguousarray(
        np.asarray(L0_ffn_b1, dtype=np.float32).reshape(32, 128).T)
    b2 = np.ascontiguousarray(
        np.asarray(L0_ffn_b2, dtype=np.float32).reshape(1, D))
    nc = _get_nc()
    in_maps = []
    for c in range(8):
        in_maps.append({"x": np.ascontiguousarray(x[c]), "wqt": wqt,
                        "wkt": wkt, "wvt": wvt, "w1t": w1t, "w2t": w2t,
                        "b1m": b1m, "b2": b2})
    res = run_bass_kernel_spmd(nc, in_maps, core_ids=list(range(8)))
    return np.stack([r["out"] for r in res.results], axis=0)
